# version 01
# baseline (speedup 1.0000x reference)
import contextlib
import numpy as np
import ml_dtypes
import concourse.bass as bass
import concourse.mybir as mybir
from concourse.bass_utils import run_bass_kernel_spmd

F32 = mybir.dt.float32
BF16 = mybir.dt.bfloat16
AF = mybir.ActivationFunctionType
ALU = mybir.AluOpType
AX = mybir.AxisListType
BF = ml_dtypes.bfloat16

D = 2048
AD = 512
KD = 16
ADT = 4
C = 128
E = 1024
ET = 8
NU = 8
HT = 16
TB = 512
EPS = 1e-5


class Sems:
    def __init__(self, nc, stack, names):
        self.h = {}
        self.c = {}
        for n in names:
            self.h[n] = stack.enter_context(nc.semaphore(n))
            self.c[n] = 0

    def dma(self, eng, dst, src, name):
        eng.dma_start(dst, src).then_inc(self.h[name], 16)
        self.c[name] += 16
        return self.c[name]

    def inc(self, inst, name):
        inst.then_inc(self.h[name], 1)
        self.c[name] += 1
        return self.c[name]

    def flush(self, eng, inst, name="fl"):
        inst.then_inc(self.h[name], 1)
        self.c[name] += 1
        eng.wait_ge(self.h[name], self.c[name])


def rep_ap(t, offset_elems, pstride, reps, inner):
    h = t.tensor if hasattr(t, "tensor") else t
    return bass.AP(h, offset_elems, [[pstride, 128], [0, reps], [1, inner]])


def build_nc(T, dbg=False):
    KIND = dict(kind="ExternalOutput") if dbg else {}
    NCH = T // C
    NB = T // TB
    nc = bass.Bass("TRN2", target_bir_lowering=False)

    xT = nc.dram_tensor("xT", [D, T], F32, kind="ExternalInput")
    wq = nc.dram_tensor("wq", [D, AD], F32, kind="ExternalInput")
    wk = nc.dram_tensor("wk", [3, D, AD], F32, kind="ExternalInput")
    wv = nc.dram_tensor("wv", [3, D, AD], F32, kind="ExternalInput")
    wo = nc.dram_tensor("wo", [AD, D], F32, kind="ExternalInput")
    wg3h = nc.dram_tensor("wg3h", [128, 48], F32, kind="ExternalInput")
    dlog = nc.dram_tensor("dlog", [1, 3], F32, kind="ExternalInput")
    oscale = nc.dram_tensor("oscale", [1, 1], F32, kind="ExternalInput")
    fgh = nc.dram_tensor("fgh", [2, 128, 128], F32, kind="ExternalInput")
    bgh = nc.dram_tensor("bgh", [1, 16], F32, kind="ExternalInput")
    fwup = nc.dram_tensor("fwup", [6, D, E], F32, kind="ExternalInput")
    fwsw = nc.dram_tensor("fwsw", [6, E, E], F32, kind="ExternalInput")
    fwdn = nc.dram_tensor("fwdn", [6, E, D], F32, kind="ExternalInput")
    fwsu = nc.dram_tensor("fwsu", [D, D], F32, kind="ExternalInput")
    fwsa = nc.dram_tensor("fwsa", [E, E], F32, kind="ExternalInput")
    fwsd = nc.dram_tensor("fwsd", [D, D], F32, kind="ExternalInput")
    biascols = nc.dram_tensor("biascols", [128, 120], F32, kind="ExternalInput")
    bexth = nc.dram_tensor("bexth", [7, D], F32, kind="ExternalInput")
    eye_bf_d = nc.dram_tensor("eye_bfd", [128, 128], BF16, kind="ExternalInput")
    eye_f_d = nc.dram_tensor("eye_fd", [128, 128], F32, kind="ExternalInput")
    ones1f_d = nc.dram_tensor("ones1fd", [1, 128], F32, kind="ExternalInput")
    onescol_d = nc.dram_tensor("onescold", [128, 1], BF16, kind="ExternalInput")
    selbf_d = nc.dram_tensor("selbfd", [8, 768], BF16, kind="ExternalInput")
    ubits_d = nc.dram_tensor("ubitsd", [128, 7], F32, kind="ExternalInput")
    c1bits_d = nc.dram_tensor("c1bitsd", [128, 7], F32, kind="ExternalInput")
    expnbits_d = nc.dram_tensor("expnbitsd", [128, 896], BF16, kind="ExternalInput")
    causT_d = nc.dram_tensor("causTd", [128, 128], BF16, kind="ExternalInput")

    y_s = nc.dram_tensor("y_s", [D, T], F32, **KIND)
    r0_s = nc.dram_tensor("r0_s", [D, T], F32, **KIND)
    r0hi_s = nc.dram_tensor("r0hi_s", [D, T], BF16, **KIND)
    r0lo_s = nc.dram_tensor("r0lo_s", [D, T], BF16, **KIND)
    xt1hi_s = nc.dram_tensor("xt1hi_s", [D, T], BF16, **KIND)
    xt1lo_s = nc.dram_tensor("xt1lo_s", [D, T], BF16, **KIND)
    wup_s = nc.dram_tensor("wup_s", [NU, D, E], BF16)
    wsw_s = nc.dram_tensor("wsw_s", [NU, E, E], BF16)
    wdn_s = nc.dram_tensor("wdn_s", [HT, 128, NU, ET, 128], BF16)
    x2T = nc.dram_tensor("x2T", [D, T], F32, kind="ExternalOutput")
    dbgout = nc.dram_tensor("dbgout", [128, 32768], F32, kind="ExternalOutput") if dbg else None

    def fm(dr):
        return dr.rearrange("(k p) t -> p k t", p=128)

    with contextlib.ExitStack() as top:
        S = Sems(nc, top, ["ld", "st", "xld", "wld", "mm", "ev", "prog", "fl"])
        sb = lambda name, shape, dt: top.enter_context(nc.sbuf_tensor(name, shape, dt))

        eye_bf = sb("eye_bf", [128, 128], BF16)
        eye_f = sb("eye_f", [128, 128], F32)
        ones1f = sb("ones1f", [1, 128], F32)
        onescol = sb("onescol", [128, 1], BF16)
        selbf = sb("selbf", [8, 768], BF16)
        wghi = sb("wghi", [128, 2, 16, 8], BF16)
        wglo = sb("wglo", [128, 2, 16, 8], BF16)
        bgb = sb("bgb", [128, 16], F32)
        bcols = sb("bcols", [128, 120], F32)
        bext_bf = sb("bext_bf", [8, D], BF16)
        osc_col = sb("osc_col", [128, 1], F32)

        with contextlib.ExitStack() as pa:
            sa = lambda name, shape, dt: pa.enter_context(nc.sbuf_tensor(name, shape, dt))
            psA_big = pa.enter_context(nc.psum_tensor("psA_big", [128, 2048], F32))
            psA_tp = pa.enter_context(nc.psum_tensor("psA_tp", [128, 1024], BF16))
            psA_sc = pa.enter_context(nc.psum_tensor("psA_sc", [128, 512], F32))
            psA_retr = pa.enter_context(nc.psum_tensor("psA_retr", [128, 512], F32))
            psA_misc = pa.enter_context(nc.psum_tensor("psA_misc", [128, 512], F32))

            wq_sb = sa("wq_sb", [128, KD, AD], BF16)
            wk_sb = sa("wk_sb", [128, 3, KD, AD], BF16)
            wv_sb = sa("wv_sb", [128, 3, KD, AD], BF16)
            wo_sb = sa("wo_sb", [128, ADT, D], BF16)
            wg3_sb = sa("wg3_sb", [128, KD, 3], BF16)
            S_sb = sa("S_sb", [128, 3, ADT, AD], BF16)
            wloc = sa("wloc", [128, 3, 128], BF16)
            cmb = sa("cmb", [128, 3, 128], BF16)
            Isc = sa("Isc", [128, 3, 128], BF16)
            wtmp = sa("wtmp", [128, 128], BF16)
            dvec = sa("dvec", [128, 12], F32)
            dvecs = [dvec[:, i : i + 1] for i in range(12)]
            decu_sb = sa("decu_sb", [128, 8], F32)
            rowd = sa("rowd", [1, 4], F32)

            with contextlib.ExitStack() as ws:
                scr_f = ws.enter_context(nc.sbuf_tensor("scr_f", [128, 160], F32))
                ubits = ws.enter_context(nc.sbuf_tensor("ubits", [128, 7], F32))
                c1bits = ws.enter_context(nc.sbuf_tensor("c1bits", [128, 7], F32))
                expnbits = ws.enter_context(nc.sbuf_tensor("expnbits", [128, 896], BF16))
                causT = ws.enter_context(nc.sbuf_tensor("causT", [128, 128], BF16))

                with nc.Block() as blk:
                    @blk.vector
                    def _(v):
                        v.memset(bext_bf[:], 0.0)
                with nc.Block() as blk:
                    @blk.sync
                    def _(e):
                        for dst, src in [
                            (eye_bf[:], eye_bf_d), (eye_f[:], eye_f_d), (ones1f[:], ones1f_d),
                            (onescol[:], onescol_d), (selbf[:], selbf_d), (ubits[:], ubits_d),
                            (c1bits[:], c1bits_d), (expnbits[:], expnbits_d), (causT[:], causT_d),
                            (bcols[:], biascols), (rowd[0:1, 0:3], dlog),
                            (scr_f[0:1, 0:1], oscale), (scr_f[0:1, 1:17], bgh),
                        ]:
                            S.dma(e, dst, src[:], "ld")

                    @blk.gpsimd
                    def _(g):
                        S.dma(g, bext_bf[0:7, :], bexth[:], "st")
                        S.dma(g, wq_sb[:], wq.rearrange("(k p) d -> p k d", p=128), "st")
                        for i in range(3):
                            S.dma(g, wk_sb[:, i], wk[i].rearrange("(k p) d -> p k d", p=128), "st")
                            S.dma(g, wv_sb[:, i], wv[i].rearrange("(k p) d -> p k d", p=128), "st")
                        S.dma(g, wo_sb[:], wo.rearrange("(k p) d -> p k d", p=128), "st")
                        S.dma(g, wg3_sb[:], wg3h.rearrange("p (k d) -> p k d", d=3), "st")
                        for u in range(NU):
                            if u < 6:
                                S.dma(g, wup_s[u].rearrange("(k p) e -> p k e", p=128),
                                      fwup[u].rearrange("(k p) e -> p k e", p=128), "st")
                                S.dma(g, wsw_s[u].rearrange("(k p) e -> p k e", p=128),
                                      fwsw[u].rearrange("(k p) e -> p k e", p=128), "st")
                                dnsrc = fwdn[u].rearrange("(k p) (m j) -> p k m j", p=128, j=128)
                            else:
                                S.dma(g, wup_s[u].rearrange("(k p) e -> p k e", p=128),
                                      fwsu.rearrange("(k p) e -> p k e", p=128)[:, :, (u - 6) * E : (u - 5) * E], "st")
                                S.dma(g, wsw_s[u].rearrange("(k p) e -> p k e", p=128),
                                      fwsa.rearrange("(k p) e -> p k e", p=128), "st")
                                dnsrc = fwsd[(u - 6) * E : (u - 5) * E, :].rearrange("(k p) (m j) -> p k m j", p=128, j=128)
                            for kk in range(ET):
                                S.dma(g, wdn_s[:, :, u, kk, :].rearrange("m p j -> p m j"),
                                      dnsrc[:, kk], "st")

                with nc.Block() as blk:
                    @blk.tensor
                    def _(t):
                        t.wait_ge(S.h["ld"], S.c["ld"])
                        t.matmul(psA_misc[:, 0:1], ones1f[:], scr_f[0:1, 0:1], start=True, stop=True)
                        t.matmul(psA_misc[:, 16:32], ones1f[:], scr_f[0:1, 1:17], start=True, stop=True)
                with nc.Block() as blk:
                    @blk.vector
                    def _(v):
                        v.tensor_copy(osc_col[:], psA_misc[:, 0:1])
                        v.tensor_copy(bgb[:], psA_misc[:, 16:32])
                with nc.Block() as blk:
                    @blk.vector
                    def _(v):
                        v.wait_ge(S.h["st"], S.c["st"])

                for j in range(2):
                    with nc.Block() as blk:
                        @blk.sync
                        def _(e, j=j):
                            S.dma(e, scr_f[:, 32:160], fgh[j], "ld")
                    with nc.Block() as blk:
                        @blk.vector
                        def _(v, j=j):
                            v.wait_ge(S.h["ld"], S.c["ld"])
                            hi = wghi[:, j].rearrange("p k d -> p (k d)")
                            lo = wglo[:, j].rearrange("p k d -> p (k d)")
                            v.tensor_copy(hi, scr_f[:, 32:160])
                            v.tensor_tensor(lo, scr_f[:, 32:160], hi, ALU.subtract)

                for i in range(3):
                    with nc.Block() as blk:
                        @blk.tensor
                        def _(t, i=i):
                            t.matmul(psA_misc[:, 32:33], ones1f[:], rowd[0:1, i : i + 1], start=True, stop=True)
                    with nc.Block() as blk:
                        @blk.scalar
                        def _(sc):
                            sc.activation(dvecs[0], psA_misc[:, 32:33], AF.Sigmoid)
                    with nc.Block() as blk:
                        @blk.vector
                        def _(v, i=i):
                            F = lambda inst: S.flush(v, inst)
                            for b in range(1, 7):
                                F(v.tensor_mul(dvecs[b], dvecs[b - 1], dvecs[b - 1]))
                            F(v.tensor_mul(dvecs[7], dvecs[6], dvecs[6]))
                            for tgt, bits in ((i, ubits), (3 + i, c1bits)):
                                cur = 8
                                for b in range(7):
                                    F(v.tensor_scalar_add(dvecs[10], dvecs[b], -1.0))
                                    F(v.tensor_scalar(dvecs[11], bits[:, b : b + 1], dvecs[10], 1.0, ALU.mult, ALU.add))
                                    if b == 0:
                                        F(v.tensor_copy(dvecs[cur], dvecs[11]))
                                    else:
                                        F(v.tensor_mul(dvecs[cur ^ 1], dvecs[cur], dvecs[11]))
                                        cur ^= 1
                                F(v.tensor_copy(decu_sb[:, tgt : tgt + 1], dvecs[cur]))
                            F(v.tensor_copy(wloc[:, i], causT[:]))
                            cur = 0
                            for b in range(7):
                                F(v.tensor_scalar_add(dvecs[10], dvecs[b], -1.0))
                                F(v.tensor_scalar(wtmp[:], expnbits[:, b * 128 : (b + 1) * 128], dvecs[10], 1.0, ALU.mult, ALU.add))
                                if cur == 0:
                                    F(v.tensor_mul(cmb[:, i], wloc[:, i], wtmp[:]))
                                else:
                                    F(v.tensor_mul(wloc[:, i], cmb[:, i], wtmp[:]))
                                cur ^= 1
                            if cur == 1:
                                F(v.tensor_copy(wloc[:, i], cmb[:, i]))
                            F(v.tensor_scalar(Isc[:, i], eye_bf[:], dvecs[7], None, ALU.mult))
                            v.memset(S_sb[:, i].rearrange("p a b -> p (a b)"), 0.0)

            xfm = sa("xfm", [128, D], F32)
            st1 = sa("st1", [128, D], BF16)
            xn = sa("xn", [128, D], BF16)
            qtok = sa("qtok", [128, AD], BF16)
            ktok = sa("ktok", [128, 3, AD], BF16)
            vtok = sa("vtok", [128, 3, AD], BF16)
            kt = sa("kt", [128, 3, AD], BF16)
            qfm = sa("qfm", [128, AD], BF16)
            kfm = sa("kfm", [128, 3, AD], BF16)
            scb = sa("scb", [128, 3, 128], BF16)
            retrbf = sa("retrbf", [128, AD], BF16)
            g3 = sa("g3", [128, 12], F32)
            gd6 = sa("gd6", [128, 8], F32)
            gdT = sa("gdT", [8, 128], BF16)
            GB = sa("GB", [128, 768], BF16)
            bc = sa("bc", [128, 256], F32)
            rows = sa("rows", [1, 896], F32)
            t9 = sa("t9", [128, D], F32)
            yb = sa("yb", [128, D], F32)
            nmax = sa("nmax", [128, 4], F32)

            RS = [rows[0:1, i * 128 : (i + 1) * 128] for i in range(7)]

            def ln_rows_blocks():
                with nc.Block() as blk:
                    @blk.vector
                    def _(v):
                        F = lambda inst: S.flush(v, inst)
                        F(v.tensor_scalar(RS[0], psA_misc[0:1, 0:128], 1.0 / D, None, ALU.mult))
                        F(v.tensor_scalar(RS[1], psA_misc[0:1, 128:256], 1.0 / D, None, ALU.mult))
                        F(v.tensor_mul(RS[2], RS[0], RS[0]))
                        F(v.tensor_sub(RS[4], RS[1], RS[2]))
                        F(v.tensor_scalar_add(RS[3], RS[4], EPS))
                with nc.Block() as blk:
                    @blk.scalar
                    def _(sc):
                        S.flush(sc, sc.activation(RS[2], RS[3], AF.Ln))
                        S.flush(sc, sc.activation(RS[4], RS[2], AF.Exp, bias=0.0, scale=-0.5))
                with nc.Block() as blk:
                    @blk.vector
                    def _(v):
                        F = lambda inst: S.flush(v, inst)
                        F(v.tensor_mul(RS[1], RS[4], RS[4]))
                        F(v.tensor_mul(RS[2], RS[1], RS[3]))
                        F(v.tensor_scalar(RS[1], RS[2], -0.5, 1.5, ALU.mult, ALU.add))
                        F(v.tensor_mul(RS[6], RS[4], RS[1]))
                        F(v.tensor_mul(RS[5], RS[0], RS[6]))
                with nc.Block() as blk:
                    @blk.tensor
                    def _(t):
                        t.matmul(psA_misc[:, 256:512], ones1f[:], rows[0:1, 640:896], start=True, stop=True)

            def stats_blocks(src_ap, extra_wait=None):
                with nc.Block() as blk:
                    @blk.scalar
                    def _(sc):
                        if extra_wait is not None:
                            extra_wait(sc)
                        sc.activation(st1[:], src_ap, AF.Identity)
                with nc.Block() as blk:
                    @blk.tensor
                    def _(t):
                        for k in range(KD):
                            t.matmul(psA_misc[0:1, 0:128], onescol[:], st1[:, k * 128 : (k + 1) * 128],
                                     start=(k == 0), stop=(k == KD - 1))
                with nc.Block() as blk:
                    @blk.scalar
                    def _(sc):
                        sc.activation(st1[:], src_ap, AF.Square)
                with nc.Block() as blk:
                    @blk.tensor
                    def _(t):
                        for k in range(KD):
                            t.matmul(psA_misc[0:1, 128:256], onescol[:], st1[:, k * 128 : (k + 1) * 128],
                                     start=(k == 0), stop=(k == KD - 1))

            store_after = {}

            def dump(slot, ap):
                if not dbg:
                    return
                with nc.Block() as blk:
                    @blk.gpsimd
                    def _(g, slot=slot, ap=ap):
                        S.dma(g, dbgout[:, slot * 2048 : slot * 2048 + _apw(ap)], ap, "st")

            def _apw(ap):
                n = 1
                for s in ap.shape[1:]:
                    n *= s
                return n

            for ci in range(NCH):
                c = NCH - 1 - ci
                xfv = xfm[:]

                with nc.Block() as blk:
                    @blk.sync
                    def _(e, ci=ci, c=c):
                        if ci >= 1:
                            e.wait_ge(S.h["prog"], ci)
                        S.dma(e, xfm[:].rearrange("p (k t) -> p k t", t=C), fm(xT)[:, :, c * C : (c + 1) * C], "xld")

                def _w1(sc, ci=ci):
                    sc.wait_ge(S.h["xld"], 16 * (ci + 1))
                    if ci >= 1:
                        sc.wait_ge(S.h["st"], store_after[ci - 1])
                stats_blocks(xfv, extra_wait=_w1)
                ln_rows_blocks()
                with nc.Block() as blk:
                    @blk.vector
                    def _(v, ci=ci):
                        S.flush(v, v.tensor_copy(bc[:], psA_misc[:, 256:512]))
                        if ci >= 1:
                            v.wait_ge(S.h["st"], store_after[ci - 1])
                        v.tensor_tensor(yb[:], xfv, rep_ap(bc, 128, 256, KD, 128), ALU.mult)
                        v.tensor_tensor(xn[:], yb[:], rep_ap(bc, 0, 256, KD, 128), ALU.subtract)

                if ci == 0:
                    dump(0, xn[:])
                projs = [("q", None), ("k", 0), ("v", 0), ("k", 1), ("v", 1), ("k", 2), ("v", 2)]
                base_mm = S.c["mm"]
                base_ev = S.c["ev"]
                with nc.Block() as blk:
                    @blk.tensor
                    def _(t, base_ev=base_ev):
                        for gi, (kind, i) in enumerate(projs):
                            if gi >= 4:
                                t.wait_ge(S.h["ev"], base_ev + gi - 3)
                            sl = psA_big[:, (gi % 4) * 512 : (gi % 4 + 1) * 512]
                            w = wq_sb[:] if kind == "q" else (wk_sb[:, i] if kind == "k" else wv_sb[:, i])
                            for k in range(KD):
                                mmi = t.matmul(sl, xn[:, k * 128 : (k + 1) * 128], w[:, k, :],
                                               start=(k == 0), stop=(k == KD - 1))
                            S.inc(mmi, "mm")
                        for k in range(KD):
                            mmi = t.matmul(psA_sc[:, 0:3], xn[:, k * 128 : (k + 1) * 128], wg3_sb[:, k, :],
                                           start=(k == 0), stop=(k == KD - 1))
                        S.inc(mmi, "mm")

                    @blk.vector
                    def _(v, base_mm=base_mm):
                        for gi, (kind, i) in enumerate(projs):
                            v.wait_ge(S.h["mm"], base_mm + gi + 1)
                            sl = psA_big[:, (gi % 4) * 512 : (gi % 4 + 1) * 512]
                            if kind == "q":
                                ev = v.tensor_copy(qtok[:], sl)
                            elif kind == "k":
                                v.tensor_copy(ktok[:, i], sl)
                                ev = v.tensor_scalar(kt[:, i], sl, decu_sb[:, i : i + 1], None, ALU.mult)
                            else:
                                ev = v.tensor_copy(vtok[:, i], sl)
                            S.inc(ev, "ev")
                        v.wait_ge(S.h["mm"], base_mm + 8)
                        v.tensor_copy(g3[:, 0:3], psA_sc[:, 0:3])

                base_mm = S.c["mm"]
                base_ev = S.c["ev"]
                with nc.Block() as blk:
                    @blk.tensor
                    def _(t, base_ev=base_ev):
                        for bi, srcs in enumerate([[("q", None), ("k", 0)], [("k", 1), ("k", 2)]]):
                            if bi == 1:
                                t.wait_ge(S.h["ev"], base_ev + 1)
                            for si, (kind, i) in enumerate(srcs):
                                src = qtok[:] if kind == "q" else ktok[:, i]
                                for jj in range(ADT):
                                    mmi = t.transpose(psA_tp[:, (si * 4 + jj) * 128 : (si * 4 + jj + 1) * 128],
                                                      src[:, jj * 128 : (jj + 1) * 128], eye_bf[:])
                            S.inc(mmi, "mm")

                    @blk.vector
                    def _(v, base_mm=base_mm):
                        v.wait_ge(S.h["mm"], base_mm + 1)
                        v.tensor_copy(qfm[:], psA_tp[:, 0:512])
                        ev = v.tensor_copy(kfm[:, 0], psA_tp[:, 512:1024])
                        S.inc(ev, "ev")
                        v.wait_ge(S.h["mm"], base_mm + 2)
                        v.tensor_copy(kfm[:, 1], psA_tp[:, 0:512])
                        v.tensor_copy(kfm[:, 2], psA_tp[:, 512:1024])

                if ci == 0:
                    dump(1, qtok[:])
                    dump(2, ktok[:, 0])
                    dump(3, vtok[:, 0])
                    dump(4, kfm[:, 0])
                    dump(5, kt[:, 0])
                with nc.Block() as blk:
                    @blk.vector
                    def _(v):
                        S.flush(v, v.reduce_max(nmax[:, 0:1], g3[:, 0:3], axis=AX.X, negate=True))
                with nc.Block() as blk:
                    @blk.scalar
                    def _(sc):
                        sc.activation(g3[:, 4:7], g3[:, 0:3], AF.Exp, bias=nmax[:, 0:1], scale=1.0)
                with nc.Block() as blk:
                    @blk.vector
                    def _(v):
                        F = lambda inst: S.flush(v, inst)
                        F(v.reduce_sum(nmax[:, 1:2], g3[:, 4:7], axis=AX.X))
                        F(v.reciprocal(nmax[:, 2:3], nmax[:, 1:2]))
                        F(v.tensor_scalar(gd6[:, 0:3], g3[:, 4:7], nmax[:, 2:3], None, ALU.mult))
                        for i in range(3):
                            F(v.tensor_scalar(gd6[:, 3 + i : 4 + i], gd6[:, i : i + 1], decu_sb[:, 3 + i : 4 + i], None, ALU.mult))
                with nc.Block() as blk:
                    @blk.tensor
                    def _(t):
                        t.transpose(psA_misc[0:6, 0:128], gd6[:, 0:6], eye_f[:])
                with nc.Block() as blk:
                    @blk.vector
                    def _(v):
                        v.tensor_copy(gdT[0:6, :], psA_misc[0:6, 0:128])

                base_mm = S.c["mm"]
                base_ev = S.c["ev"]
                with nc.Block() as blk:
                    @blk.tensor
                    def _(t, base_ev=base_ev):
                        for n in range(4):
                            mmi = t.matmul(psA_big[:, n * 128 : (n + 1) * 128],
                                           selbf[0:6, n * 128 : (n + 1) * 128], gdT[0:6, :], start=True, stop=True)
                        S.inc(mmi, "mm")
                        t.wait_ge(S.h["ev"], base_ev + 1)
                        for n in range(4, 6):
                            mmi = t.matmul(psA_big[:, (n - 4) * 128 : (n - 3) * 128],
                                           selbf[0:6, n * 128 : (n + 1) * 128], gdT[0:6, :], start=True, stop=True)
                        S.inc(mmi, "mm")

                    @blk.vector
                    def _(v, base_mm=base_mm):
                        v.wait_ge(S.h["mm"], base_mm + 1)
                        ev = v.tensor_copy(GB[:, 0:512], psA_big[:, 0:512])
                        S.inc(ev, "ev")
                        v.wait_ge(S.h["mm"], base_mm + 2)
                        v.tensor_copy(GB[:, 512:768], psA_big[:, 0:256])

                if ci == 0:
                    dump(6, g3[:])
                    dump(7, GB[:])
                with nc.Block() as blk:
                    @blk.vector
                    def _(v):
                        for i in range(3):
                            v.tensor_tensor(cmb[:, i], wloc[:, i], GB[:, i * 128 : (i + 1) * 128], ALU.mult)
                            v.tensor_tensor(ktok[:, i], qfm[:], rep_ap(GB, 384 + i * 128, 768, ADT, 128), ALU.mult)

                first_retr = [True]
                for i in range(3):
                    base_mm = S.c["mm"]
                    with nc.Block() as blk:
                        @blk.tensor
                        def _(t, i=i):
                            for kd in range(ADT):
                                mmi = t.matmul(psA_sc[:, 128:256], kfm[:, i, kd * 128 : (kd + 1) * 128],
                                               qfm[:, kd * 128 : (kd + 1) * 128], start=(kd == 0), stop=(kd == ADT - 1))
                            S.inc(mmi, "mm")
                    with nc.Block() as blk:
                        @blk.vector
                        def _(v, i=i, base_mm=base_mm):
                            v.wait_ge(S.h["mm"], base_mm + 1)
                            v.tensor_tensor(scb[:, i], psA_sc[:, 128:256], cmb[:, i], ALU.mult)
                    with nc.Block() as blk:
                        @blk.tensor
                        def _(t, i=i):
                            for m in range(ADT):
                                t.matmul(psA_retr[:, m * 128 : (m + 1) * 128], vtok[:, i, m * 128 : (m + 1) * 128],
                                         scb[:, i], start=first_retr[0], stop=False)
                                first_retr[0] = False
                            for m in range(ADT):
                                for kd in range(ADT):
                                    last = (i == 2 and m == ADT - 1 and kd == ADT - 1)
                                    t.matmul(psA_retr[:, m * 128 : (m + 1) * 128],
                                             S_sb[:, i, kd, m * 128 : (m + 1) * 128],
                                             ktok[:, i, kd * 128 : (kd + 1) * 128], start=False, stop=last)
                with nc.Block() as blk:
                    @blk.vector
                    def _(v):
                        v.tensor_scalar(retrbf[:], psA_retr[:], osc_col[:], None, ALU.mult)

                if ci == 0:
                    dump(8, retrbf[:])
                    dump(9, scb[:].rearrange("p a b -> p (a b)"))
                    dump(11, wloc[:].rearrange("p a b -> p (a b)"))
                    dump(12, decu_sb[:])
                for i in range(3):
                    base_mm = S.c["mm"]
                    with nc.Block() as blk:
                        @blk.tensor
                        def _(t, i=i):
                            for m in range(ADT):
                                t.matmul(psA_big[:, m * 512 : (m + 1) * 512], kt[:, i, m * 128 : (m + 1) * 128],
                                         vtok[:, i], start=True, stop=False)
                                mmi = t.matmul(psA_big[:, m * 512 : (m + 1) * 512], Isc[:, i],
                                               S_sb[:, i, m, :], start=False, stop=True)
                            S.inc(mmi, "mm")
                    with nc.Block() as blk:
                        @blk.vector
                        def _(v, i=i, base_mm=base_mm):
                            v.wait_ge(S.h["mm"], base_mm + 1)
                            v.tensor_copy(S_sb[:, i].rearrange("p a b -> p (a b)"), psA_big[:])

                if ci == 0:
                    dump(13, S_sb[:, 0].rearrange("p a b -> p (a b)"))
                with nc.Block() as blk:
                    @blk.tensor
                    def _(t):
                        for m in range(HT):
                            for kd in range(ADT):
                                t.matmul(psA_big[:, m * 128 : (m + 1) * 128], wo_sb[:, kd, m * 128 : (m + 1) * 128],
                                         retrbf[:, kd * 128 : (kd + 1) * 128], start=(kd == 0), stop=(kd == ADT - 1))

                stats_blocks(psA_big[:])
                ln_rows_blocks()
                with nc.Block() as blk:
                    @blk.vector
                    def _(v, ci=ci):
                        S.flush(v, v.tensor_copy(bc[:], psA_misc[:, 256:512]))
                        if ci >= 1:
                            v.wait_ge(S.h["st"], store_after[ci - 1])
                        v.tensor_tensor(t9[:], psA_big[:], rep_ap(bc, 128, 256, KD, 128), ALU.mult)
                        v.tensor_tensor(t9[:], t9[:], rep_ap(bc, 0, 256, KD, 128), ALU.subtract)
                with nc.Block() as blk:
                    @blk.gpsimd
                    def _(g):
                        g.tensor_tensor(yb[:], t9[:], xfv, ALU.add).then_inc(S.h["prog"], 1)
                        S.c["prog"] += 1

                stats_blocks(yb[:])
                ln_rows_blocks()
                with nc.Block() as blk:
                    @blk.vector
                    def _(v):
                        S.flush(v, v.tensor_copy(bc[:], psA_misc[:, 256:512]))
                        v.tensor_tensor(t9[:], yb[:], rep_ap(bc, 128, 256, KD, 128), ALU.mult)
                        v.tensor_tensor(t9[:], t9[:], rep_ap(bc, 0, 256, KD, 128), ALU.subtract)
                        v.tensor_copy(xn[:], t9[:])
                        v.tensor_tensor(st1[:], t9[:], xn[:], ALU.subtract)

                if ci == 0:
                    dump(14, t9[:])
                    dump(15, yb[:])
                with nc.Block() as blk:
                    @blk.gpsimd
                    def _(g, c=c, ci=ci):
                        dst = lambda dr: fm(dr)[:, :, c * C : (c + 1) * C]
                        v3 = lambda t_: t_.rearrange("p (k t) -> p k t", t=C)
                        S.dma(g, dst(y_s), v3(yb[:]), "st")
                        S.dma(g, dst(r0_s), v3(t9[:]), "st")
                        S.dma(g, dst(r0hi_s), v3(xn[:]), "st")
                        S.dma(g, dst(r0lo_s), v3(st1[:]), "st")
                        store_after[ci] = S.c["st"]

        with contextlib.ExitStack() as pb:
            sbb = lambda name, shape, dt: pb.enter_context(nc.sbuf_tensor(name, shape, dt))
            psB_big = pb.enter_context(nc.psum_tensor("psB_big", [128, 2048], F32))
            psB_w = pb.enter_context(nc.psum_tensor("psB_w", [128, 512], F32))
            psB_g = pb.enter_context(nc.psum_tensor("psB_g", [128, 512], F32))
            psB_s1 = pb.enter_context(nc.psum_tensor("psB_s1", [128, 512], F32))
            psB_s2 = pb.enter_context(nc.psum_tensor("psB_s2", [128, 512], F32))

            wub = sbb("wub", [128, 2, KD, 512], BF16)
            wswb = sbb("wswb", [128, 2, ET, 512], BF16)
            wdnb = sbb("wdnb", [128, 2, 4, ET, 128], BF16)
            xtu = sbb("xtu", [128, KD, TB], BF16)
            xtlo = sbb("xtlo", [128, KD, 128], BF16)
            hst = sbb("hst", [128, NU, ET, TB], BF16)
            pre = sbb("pre", [128, ET, TB], BF16)
            ssw = sbb("ssw", [128, 4, TB], BF16)
            wText = sbb("wText", [8, T], BF16)
            r0t = sbb("r0t", [128, 2, 512], F32)
            wbn = sbb("wbn", [128, TB], BF16)
            zsq = sbb("zsq", [128, 2, 512], BF16)
            zlold = sbb("zlold", [128, 2, 512], BF16)
            ytl = sbb("ytl", [128, 2, 512], F32)
            stage = sbb("stage", [128, 4, 512], F32)
            sthi = sbb("sthi", [128, 2, 512], BF16)
            stlo = sbb("stlo", [128, 2, 512], BF16)
            gg = sbb("gg", [128, 56], F32)
            browsB = sbb("browsB", [1, 1536], F32)

            for j in range(2):
                hi_src = r0hi_s if j == 0 else xt1hi_s
                lo_src = r0lo_s if j == 0 else xt1lo_s
                round_store_base = S.c["st"]

                with nc.Block() as blk:
                    @blk.vector
                    def _(v):
                        v.memset(wText[:], 1.0)
                for g in range(NB):
                    with nc.Block() as blk:
                        @blk.sync
                        def _(e, g=g, round_store_base=round_store_base):
                            e.wait_ge(S.h["st"], round_store_base)
                            S.dma(e, xtu[:], fm(hi_src)[:, :, g * TB : (g + 1) * TB], "ld")
                    ld_hi = S.c["ld"]
                    for s_ in range(4):
                        with nc.Block() as blk:
                            @blk.sync
                            def _(e, g=g, s_=s_):
                                S.dma(e, xtlo[:],
                                      fm(lo_src)[:, :, g * TB + s_ * 128 : g * TB + (s_ + 1) * 128], "ld")
                        ld_lo = S.c["ld"]
                        with nc.Block() as blk:
                            @blk.tensor
                            def _(t, s_=s_, j=j, ld_lo=ld_lo):
                                t.wait_ge(S.h["ld"], ld_lo)
                                ps = psB_g[:, 0:8]
                                for k in range(KD):
                                    t.matmul(ps, xtu[:, k, s_ * 128 : (s_ + 1) * 128], wghi[:, j, k, :],
                                             start=(k == 0), stop=False)
                                for k in range(KD):
                                    t.matmul(ps, xtu[:, k, s_ * 128 : (s_ + 1) * 128], wglo[:, j, k, :],
                                             start=False, stop=False)
                                for k in range(KD):
                                    t.matmul(ps, xtlo[:, k, :], wghi[:, j, k, :],
                                             start=False, stop=(k == KD - 1))
                        with nc.Block() as blk:
                            @blk.vector
                            def _(v, j=j):
                                F = lambda inst: S.flush(v, inst)
                                G8 = gg[:, 0:8]
                                F(v.tensor_tensor(G8, psB_g[:, 0:8], bgb[:, j * 8 : (j + 1) * 8], ALU.add))
                                cur = 8
                                for m in range(8):
                                    if m == 0:
                                        F(v.tensor_single_scalar(gg[:, 8:16], G8, G8[:, 0:1], ALU.is_lt))
                                    else:
                                        F(v.tensor_single_scalar(gg[:, 24:32], G8, G8[:, m : m + 1], ALU.is_lt))
                                        nxt = 24 - cur
                                        F(v.tensor_add(gg[:, nxt : nxt + 8], gg[:, cur : cur + 8], gg[:, 24:32]))
                                        cur = nxt
                                F(v.tensor_single_scalar(gg[:, 32:40], gg[:, cur : cur + 8], 5.5, ALU.is_lt))
                                F(v.reduce_max(gg[:, 48:49], G8, axis=AX.X, negate=True))
                        with nc.Block() as blk:
                            @blk.scalar
                            def _(sc):
                                sc.activation(gg[:, 40:48], gg[:, 0:8], AF.Exp, bias=gg[:, 48:49], scale=1.0)
                        with nc.Block() as blk:
                            @blk.vector
                            def _(v):
                                F = lambda inst: S.flush(v, inst)
                                F(v.tensor_mul(gg[:, 24:32], gg[:, 40:48], gg[:, 32:40]))
                                F(v.reduce_sum(gg[:, 49:50], gg[:, 24:32], axis=AX.X))
                                F(v.reciprocal(gg[:, 50:51], gg[:, 49:50]))
                                F(v.tensor_scalar(gg[:, 0:8], gg[:, 24:32], gg[:, 50:51], None, ALU.mult))
                        with nc.Block() as blk:
                            @blk.tensor
                            def _(t):
                                t.transpose(psB_g[0:8, 128:256], gg[:, 0:8], eye_f[:])
                        with nc.Block() as blk:
                            @blk.vector
                            def _(v, g=g, s_=s_):
                                v.tensor_copy(wText[0:6, g * TB + s_ * 128 : g * TB + (s_ + 1) * 128],
                                              psB_g[0:6, 128:256])

                for g in range(NB):
                    with nc.Block() as blk:
                        @blk.sync
                        def _(e, g=g):
                            S.dma(e, xtu[:], fm(hi_src)[:, :, g * TB : (g + 1) * TB], "ld")
                            for h in range(2):
                                S.dma(e, wub[:, h],
                                      wup_s[0].rearrange("(k p) e -> p k e", p=128)[:, :, h * 512 : (h + 1) * 512], "wld")
                                S.dma(e, wswb[:, h],
                                      wsw_s[0].rearrange("(k p) e -> p k e", p=128)[:, :, h * 512 : (h + 1) * 512], "wld")
                    xtu_ld = S.c["ld"]
                    wld_up = {(0, 0): S.c["wld"], (0, 1): S.c["wld"]}
                    wld_sw = {(0, 0): S.c["wld"], (0, 1): S.c["wld"]}

                    for n in range(NU):
                        if n < 6:
                            base_mm = S.c["mm"]
                            with nc.Block() as blk:
                                @blk.tensor
                                def _(t, n=n, g=g):
                                    mmi = t.matmul(psB_w[:], selbf[0:8, n * 128 : (n + 1) * 128],
                                                   wText[0:8, g * TB : (g + 1) * TB], start=True, stop=True)
                                    S.inc(mmi, "mm")
                                @blk.vector
                                def _(v, base_mm=base_mm):
                                    v.wait_ge(S.h["mm"], base_mm + 1)
                                    v.tensor_copy(wbn[:], psB_w[:])

                        base_mm = S.c["mm"]
                        base_ev = S.c["ev"]
                        with nc.Block() as blk:
                            @blk.sync
                            def _(e, n=n, base_mm=base_mm):
                                if n + 1 < NU:
                                    for h in range(2):
                                        e.wait_ge(S.h["mm"], base_mm + (h + 1) * 4)
                                        S.dma(e, wub[:, h],
                                              wup_s[n + 1].rearrange("(k p) e -> p k e", p=128)[:, :, h * 512 : (h + 1) * 512], "wld")
                                        wld_up[(n + 1, h)] = S.c["wld"]

                            @blk.tensor
                            def _(t, n=n, base_ev=base_ev, xtu_ld=xtu_ld):
                                t.wait_ge(S.h["ld"], xtu_ld)
                                cnt = 0
                                for half in range(2):
                                    t.wait_ge(S.h["wld"], wld_up[(n, half)])
                                    for lm in range(4):
                                        m = half * 4 + lm
                                        if cnt >= 4:
                                            t.wait_ge(S.h["ev"], base_ev + cnt - 3)
                                        sl = psB_big[:, (m % 4) * 512 : (m % 4 + 1) * 512]
                                        for k in range(KD):
                                            mmi = t.matmul(sl, wub[:, half, k, lm * 128 : (lm + 1) * 128],
                                                           xtu[:, k, :], start=(k == 0), stop=(k == KD - 1))
                                        S.inc(mmi, "mm")
                                        cnt += 1

                            @blk.scalar
                            def _(sc, n=n, base_mm=base_mm):
                                for m in range(ET):
                                    sc.wait_ge(S.h["mm"], base_mm + m + 1)
                                    sl = psB_big[:, (m % 4) * 512 : (m % 4 + 1) * 512]
                                    if n < 6:
                                        bcol = bcols[:, n * 8 + m : n * 8 + m + 1]
                                    else:
                                        bcol = bcols[:, 96 + (n - 6) * 8 + m : 96 + (n - 6) * 8 + m + 1]
                                    ev = sc.activation(pre[:, m, :], sl, AF.Identity, bias=bcol, scale=1.0)
                                    S.inc(ev, "ev")

                        base_mm = S.c["mm"]
                        base_ev = S.c["ev"]
                        base_pg = S.c["prog"]
                        with nc.Block() as blk:
                            @blk.sync
                            def _(e, n=n, base_mm=base_mm):
                                if n + 1 < NU:
                                    for h in range(2):
                                        e.wait_ge(S.h["mm"], base_mm + (h + 1) * 4)
                                        S.dma(e, wswb[:, h],
                                              wsw_s[n + 1].rearrange("(k p) e -> p k e", p=128)[:, :, h * 512 : (h + 1) * 512], "wld")
                                        wld_sw[(n + 1, h)] = S.c["wld"]

                            @blk.tensor
                            def _(t, n=n, base_ev=base_ev):
                                cnt = 0
                                for half in range(2):
                                    t.wait_ge(S.h["wld"], wld_sw[(n, half)])
                                    for lm in range(4):
                                        m = half * 4 + lm
                                        if cnt >= 4:
                                            t.wait_ge(S.h["ev"], base_ev + cnt - 3)
                                        sl = psB_big[:, (m % 4) * 512 : (m % 4 + 1) * 512]
                                        for k in range(ET):
                                            mmi = t.matmul(sl, wswb[:, half, k, lm * 128 : (lm + 1) * 128],
                                                           pre[:, k, :], start=(k == 0), stop=(k == ET - 1))
                                        S.inc(mmi, "mm")
                                        cnt += 1

                            @blk.scalar
                            def _(sc, n=n, base_mm=base_mm, base_pg=base_pg):
                                for m in range(ET):
                                    sc.wait_ge(S.h["mm"], base_mm + m + 1)
                                    if m >= 4:
                                        sc.wait_ge(S.h["prog"], base_pg + 1)
                                    sl = psB_big[:, (m % 4) * 512 : (m % 4 + 1) * 512]
                                    if n < 6:
                                        bcol = bcols[:, 48 + n * 8 + m : 48 + n * 8 + m + 1]
                                    else:
                                        bcol = bcols[:, 112 + m : 112 + m + 1]
                                    ev = sc.activation(ssw[:, m % 4, :], sl, AF.Silu, bias=bcol, scale=1.0)
                                    S.inc(ev, "ev")

                            @blk.vector
                            def _(v, n=n, base_ev=base_ev):
                                for half in range(2):
                                    v.wait_ge(S.h["ev"], base_ev + (half + 1) * 4)
                                    so = ssw[:].rearrange("p a t -> p (a t)")
                                    po = pre[:, half * 4 : (half + 1) * 4, :].rearrange("p a t -> p (a t)")
                                    ho = hst[:, n, half * 4 : (half + 1) * 4, :].rearrange("p a t -> p (a t)")
                                    pg = v.tensor_tensor(ho, so, po, ALU.mult)
                                    if n < 6:
                                        pg = v.tensor_tensor(ho, ho, rep_ap(wbn, 0, TB, 4, TB), ALU.mult)
                                    pg.then_inc(S.h["prog"], 1)
                                    S.c["prog"] += 1

                    base_mm = S.c["mm"]
                    base_ev = S.c["ev"]
                    st_pre = S.c["st"]
                    wdn_ld = {}
                    r0_ld = {}
                    with nc.Block() as blk:
                        @blk.sync
                        def _(e, g=g, base_mm=base_mm, base_ev=base_ev):
                            for m in range(HT):
                                for uh in range(2):
                                    if m >= 1:
                                        e.wait_ge(S.h["mm"], base_mm + m)
                                    S.dma(e, wdnb[:, uh],
                                          wdn_s[m, :, uh * 4 : (uh + 1) * 4, :, :], "wld")
                                    wdn_ld[(m, uh)] = S.c["wld"]
                                if m >= 2:
                                    e.wait_ge(S.h["ev"], base_ev + m - 1)
                                S.dma(e, r0t[:, m % 2, :], fm(r0_s)[:, m, g * TB : (g + 1) * TB], "ld")
                                r0_ld[m] = S.c["ld"]

                        @blk.tensor
                        def _(t, g=g, base_ev=base_ev):
                            for m in range(HT):
                                if m >= 4:
                                    t.wait_ge(S.h["ev"], base_ev + m - 3)
                                sl = psB_big[:, (m % 4) * 512 : (m % 4 + 1) * 512]
                                first = True
                                for uh in range(2):
                                    t.wait_ge(S.h["wld"], wdn_ld[(m, uh)])
                                    for lu in range(4):
                                        u = uh * 4 + lu
                                        for k in range(ET):
                                            t.matmul(sl, wdnb[:, uh, lu, k, :], hst[:, u, k, :],
                                                     start=first, stop=False)
                                            first = False
                                mmi = t.matmul(sl, bext_bf[0:7, m * 128 : (m + 1) * 128],
                                               wText[0:7, g * TB : (g + 1) * TB], start=False, stop=True)
                                S.inc(mmi, "mm")

                        @blk.vector
                        def _(v, g=g, j=j, base_mm=base_mm, st_pre=st_pre):
                            for m in range(HT):
                                v.wait_ge(S.h["mm"], base_mm + m + 1)
                                v.wait_ge(S.h["ld"], r0_ld[m])
                                if m >= 2:
                                    v.wait_ge(S.h["st"], st_pre + 32 * (m - 1))
                                sl = psB_big[:, (m % 4) * 512 : (m % 4 + 1) * 512]
                                zt = stage[:, m % 4, :]
                                S.flush(v, v.tensor_tensor(zt, sl, r0t[:, m % 2, :], ALU.add))
                                if j == 0:
                                    S.flush(v, v.tensor_copy(sthi[:, m % 2, :], zt))
                                    ev = v.tensor_tensor(stlo[:, m % 2, :], zt, sthi[:, m % 2, :], ALU.subtract)
                                else:
                                    S.flush(v, v.tensor_copy(xtu[:, m, :], zt))
                                    S.flush(v, v.tensor_copy(sthi[:, m % 2, :], xtu[:, m, :]))
                                    ev = v.tensor_tensor(stlo[:, m % 2, :], zt, xtu[:, m, :], ALU.subtract)
                                S.inc(ev, "ev")

                        @blk.gpsimd
                        def _(gp, g=g, base_ev=base_ev):
                            for m in range(HT):
                                gp.wait_ge(S.h["ev"], base_ev + m + 1)
                                dst = lambda dr: fm(dr)[:, m, g * TB : (g + 1) * TB]
                                S.dma(gp, dst(xt1hi_s), sthi[:, m % 2, :], "st")
                                S.dma(gp, dst(xt1lo_s), stlo[:, m % 2, :], "st")

                    if j == 1:
                        base_mm = S.c["mm"]
                        base_ev = S.c["ev"]
                        lo_ldv = {}
                        with nc.Block() as blk:
                            @blk.sync
                            def _(e, g=g, base_mm=base_mm):
                                for m in range(HT):
                                    if m >= 2:
                                        e.wait_ge(S.h["mm"], base_mm + m - 1)
                                    S.dma(e, zlold[:, m % 2, :], fm(xt1lo_s)[:, m, g * TB : (g + 1) * TB], "ld")
                                    lo_ldv[m] = S.c["ld"]

                            @blk.scalar
                            def _(sc, base_mm=base_mm):
                                for m in range(HT):
                                    if m >= 2:
                                        sc.wait_ge(S.h["mm"], base_mm + m - 1)
                                    ev = sc.activation(zsq[:, m % 2, :], xtu[:, m, :], AF.Square)
                                    S.inc(ev, "ev")

                            @blk.tensor
                            def _(t, base_ev=base_ev):
                                for m in range(HT):
                                    t.wait_ge(S.h["ld"], lo_ldv[m])
                                    t.wait_ge(S.h["ev"], base_ev + m + 1)
                                    t.matmul(psB_s1[0:1, 0:512], onescol[:], xtu[:, m, :],
                                             start=(m == 0), stop=False)
                                    t.matmul(psB_s1[0:1, 0:512], onescol[:], zlold[:, m % 2, :],
                                             start=False, stop=(m == HT - 1))
                                    mmi = t.matmul(psB_s2[0:1, 0:512], onescol[:], zsq[:, m % 2, :],
                                                   start=(m == 0), stop=(m == HT - 1))
                                    S.inc(mmi, "mm")

                        t0 = stage[0:1, 0, 0:512]; t1 = stage[0:1, 1, 0:512]
                        t2 = stage[0:1, 2, 0:512]; t3 = stage[0:1, 3, 0:512]
                        b0 = browsB[0:1, 0:512]; b1 = browsB[0:1, 512:1024]
                        b2 = browsB[0:1, 1024:1536]
                        with nc.Block() as blk:
                            @blk.vector
                            def _(v):
                                F = lambda inst: S.flush(v, inst)
                                F(v.tensor_scalar(t0, psB_s1[0:1, 0:512], 1.0 / D, None, ALU.mult))
                                F(v.tensor_scalar(t1, psB_s2[0:1, 0:512], 1.0 / D, None, ALU.mult))
                                F(v.tensor_mul(t2, t0, t0))
                                F(v.tensor_sub(t3, t1, t2))
                                F(v.tensor_scalar_add(b2, t3, EPS))
                        with nc.Block() as blk:
                            @blk.scalar
                            def _(sc):
                                S.flush(sc, sc.activation(t1, b2, AF.Ln))
                                S.flush(sc, sc.activation(t2, t1, AF.Exp, bias=0.0, scale=-0.5))
                        with nc.Block() as blk:
                            @blk.vector
                            def _(v):
                                F = lambda inst: S.flush(v, inst)
                                F(v.tensor_mul(t1, t2, t2))
                                F(v.tensor_mul(t3, t1, b2))
                                F(v.tensor_scalar(t1, t3, -0.5, 1.5, ALU.mult, ALU.add))
                                F(v.tensor_mul(b1, t2, t1))
                                F(v.tensor_mul(b0, t0, b1))
                        with nc.Block() as blk:
                            @blk.tensor
                            def _(t):
                                t.matmul(psB_s1[:, 0:512], ones1f[:], b0, start=True, stop=True)
                                t.matmul(psB_s2[:, 0:512], ones1f[:], b1, start=True, stop=True)

                        base_ev = S.c["ev"]
                        st_pre2 = S.c["st"]
                        y_ldv = {}
                        lo_ldv2 = {}
                        with nc.Block() as blk:
                            @blk.sync
                            def _(e, g=g, base_ev=base_ev):
                                for m in range(HT):
                                    if m >= 2:
                                        e.wait_ge(S.h["ev"], base_ev + m - 1)
                                    S.dma(e, ytl[:, m % 2, :], fm(y_s)[:, m, g * TB : (g + 1) * TB], "ld")
                                    y_ldv[m] = S.c["ld"]
                                    S.dma(e, zlold[:, m % 2, :], fm(xt1lo_s)[:, m, g * TB : (g + 1) * TB], "ld")
                                    lo_ldv2[m] = S.c["ld"]

                            @blk.vector
                            def _(v, g=g, base_ev=base_ev, st_pre2=st_pre2):
                                for m in range(HT):
                                    v.wait_ge(S.h["ld"], lo_ldv2[m])
                                    if m >= 2:
                                        v.wait_ge(S.h["st"], st_pre2 + 16 * (m - 1))
                                    sA = stage[:, 2 * (m % 2), :]
                                    sB = stage[:, 2 * (m % 2) + 1, :]
                                    S.flush(v, v.tensor_tensor(sA, xtu[:, m, :], zlold[:, m % 2, :], ALU.add))
                                    S.flush(v, v.tensor_mul(sB, sA, psB_s2[:, 0:512]))
                                    S.flush(v, v.tensor_sub(sA, sB, psB_s1[:, 0:512]))
                                    ev = v.tensor_tensor(sB, sA, ytl[:, m % 2, :], ALU.add)
                                    S.inc(ev, "ev")

                            @blk.gpsimd
                            def _(gp, g=g, base_ev=base_ev):
                                for m in range(HT):
                                    gp.wait_ge(S.h["ev"], base_ev + m + 1)
                                    S.dma(gp, fm(x2T)[:, m, g * TB : (g + 1) * TB], stage[:, 2 * (m % 2) + 1, :], "st")

            with nc.Block() as blk:
                @blk.gpsimd
                def _(gp):
                    gp.wait_ge(S.h["st"], S.c["st"])

    return nc


def _consts():
    u = np.arange(128)
    ubits = ((u[:, None] >> np.arange(7)[None, :]) & 1).astype(np.float32)
    c1 = 127 - u
    c1bits = ((c1[:, None] >> np.arange(7)[None, :]) & 1).astype(np.float32)
    s = np.arange(128)[:, None]
    t = np.arange(128)[None, :]
    expn = np.maximum(s - t - 1, 0)
    expnbits = np.concatenate([((expn >> b) & 1).astype(np.float32) for b in range(7)], axis=1).astype(BF)
    causT = (s > t).astype(np.float32).astype(BF)
    sel = np.zeros((8, 768), np.float32)
    for n in range(6):
        sel[n, n * 128 : (n + 1) * 128] = 1.0
    return dict(
        eye_bfd=np.eye(128, dtype=np.float32).astype(BF),
        eye_fd=np.eye(128, dtype=np.float32),
        ones1fd=np.ones((1, 128), np.float32),
        onescold=np.ones((128, 1), np.float32).astype(BF),
        selbfd=sel.astype(BF),
        ubitsd=ubits, c1bitsd=c1bits, expnbitsd=expnbits, causTd=causT,
    )


def _prep_inputs(inputs, T):
    f = lambda k: np.ascontiguousarray(np.asarray(inputs[k], np.float32))
    wg3 = f("L0_attn_Wg")
    wg3h = np.ascontiguousarray(wg3.reshape(16, 128, 3).transpose(1, 0, 2).reshape(128, 48))
    fg = f("L0_ffn_Wg")
    fgh = np.ascontiguousarray(fg.reshape(2, 16, 128, 8).transpose(0, 2, 1, 3).reshape(2, 128, 128))
    bup = f("L0_ffn_bup").reshape(6, 8, 128).transpose(2, 0, 1).reshape(128, 48)
    bsw = f("L0_ffn_bsw").reshape(6, 8, 128).transpose(2, 0, 1).reshape(128, 48)
    bsu = f("L0_ffn_bsu").reshape(2, 8, 128).transpose(2, 0, 1).reshape(128, 16)
    bsa = f("L0_ffn_bsa").reshape(8, 128).transpose(1, 0).reshape(128, 8)
    biascols = np.ascontiguousarray(np.concatenate([bup, bsw, bsu, bsa], axis=1))
    bext = np.ascontiguousarray(np.concatenate([f("L0_ffn_bdn"), f("L0_ffn_bsd").reshape(1, -1)], 0))
    common = dict(
        wq=f("L0_attn_Wq"), wk=f("L0_attn_Wk"), wv=f("L0_attn_Wv"), wo=f("L0_attn_Wo"),
        wg3h=wg3h, dlog=f("L0_attn_decay_logits").reshape(1, 3),
        oscale=np.asarray(inputs["L0_attn_out_scale"], np.float32).reshape(1, 1),
        fgh=fgh, bgh=f("L0_ffn_bg").reshape(1, 16),
        fwup=f("L0_ffn_Wup"), fwsw=f("L0_ffn_Wsw"), fwdn=f("L0_ffn_Wdn"),
        fwsu=f("L0_ffn_Wsu"), fwsa=f("L0_ffn_Wsa"), fwsd=f("L0_ffn_Wsd"),
        biascols=biascols, bexth=bext,
    )
    common.update(_consts())
    x = np.asarray(inputs["L0_attn_x"], np.float32)
    maps = []
    for b in range(x.shape[0]):
        m = dict(common)
        m["xT"] = np.ascontiguousarray(x[b, :T].T)
        maps.append(m)
    return maps


def kernel(**inputs):
    T = 4096
    nc = build_nc(T)
    in_maps = _prep_inputs(inputs, T)
    res = run_bass_kernel_spmd(nc, in_maps, core_ids=list(range(8)))
    out = np.stack([np.ascontiguousarray(np.asarray(r["x2T"]).T) for r in res.results], 0)
    return out.astype(np.float32)
